# version 02
# baseline (speedup 1.0000x reference)
import numpy as np
import ml_dtypes
from contextlib import ExitStack

import concourse.bass as bass
import concourse.mybir as mybir
from concourse.bass_utils import run_bass_kernel_spmd

F32 = mybir.dt.float32
BF16 = mybir.dt.bfloat16
U8 = mybir.dt.uint8
AF = mybir.ActivationFunctionType
ALU = mybir.AluOpType
BFT = ml_dtypes.bfloat16

P = 128
T = 1024
S = 2048
D = 4096
NCH = 32
NRES = 16
ENGINES = ("tensor", "vector", "scalar", "gpsimd", "sync")
RS512 = 0.04419417382415922
RG = [[0, 1], [2, 3], [4, 5], [6, 7]]


def _tiles_T(W, bias=None):
    O, K = W.shape
    t = np.ascontiguousarray(W.reshape(O // 128, 128, K).transpose(0, 2, 1))
    if bias is not None:
        t = np.concatenate([t, bias.reshape(O // 128, 1, 128)], axis=1)
    return np.ascontiguousarray(t)


def _conv1_slab(w, b):
    C, Cg, _ = w.shape
    nch = C // 128
    out = np.zeros((nch, 3 * 128 + 1, 128), np.float32)
    for c in range(nch):
        for j in range(3):
            for gg in range(128 // Cg):
                r0 = gg * Cg
                blk = w[c * 128 + r0:c * 128 + r0 + Cg, :, j]
                out[c, j * 128 + r0:j * 128 + r0 + Cg, r0:r0 + Cg] = blk.T
        out[c, 384, :] = b[c * 128:(c + 1) * 128]
    return out


def _conv2_slab(w, b):
    C = w.shape[0]
    nch = C // 128
    out = np.zeros((nch, 3 * nch * 128 + 1, 128), np.float32)
    for mc in range(nch):
        for j in range(3):
            for kc in range(nch):
                blk = w[mc * 128:(mc + 1) * 128, kc * 128:(kc + 1) * 128, j]
                out[mc, (j * nch + kc) * 128:(j * nch + kc + 1) * 128, :] = blk.T
        out[mc, 3 * nch * 128, :] = b[mc * 128:(mc + 1) * 128]
    return out


def host_prepare(inputs):
    d = inputs
    act = np.asarray(d["L0_ffn_active"]).astype(np.int64)

    Wqk = np.asarray(d["L0_attn_Wqk_w"], np.float32)
    bqk = np.asarray(d["L0_attn_Wqk_b"], np.float32)
    Wq = np.ascontiguousarray(Wqk.reshape(8, 64, 2, D)[:, :, 0, :].reshape(512, D))
    Wk = np.ascontiguousarray(Wqk.reshape(8, 64, 2, D)[:, :, 1, :].reshape(512, D))
    bq = np.ascontiguousarray(bqk.reshape(8, 64, 2)[:, :, 0].reshape(512))
    bk = np.ascontiguousarray(bqk.reshape(8, 64, 2)[:, :, 1].reshape(512))
    wqkv = np.asarray(d["L1_attn_w_qkv"], np.float32)

    A0 = np.asarray(d["L0_ffn_lora_A"], np.float32)[act].reshape(32, D)
    B0 = np.asarray(d["L0_ffn_lora_B"], np.float32)[act]
    B0s = np.ascontiguousarray(B0.transpose(1, 0, 2).reshape(D, 32))
    rW = np.asarray(d["L0_ffn_router_W"], np.float32)[act]
    A1s = np.ascontiguousarray(
        np.asarray(d["L1_ffn_A"], np.float32).transpose(0, 2, 1).reshape(P, D))
    B1 = np.asarray(d["L1_ffn_B"], np.float32).reshape(P, D)
    Wvf = np.asarray(d["L0_attn_Wv_w"], np.float32)
    bvf = np.asarray(d["L0_attn_Wv_b"], np.float32)

    shared = {
        "l0_wq_t": _tiles_T(Wq, bq),
        "l0_wk_t": _tiles_T(Wk, bk),
        "l0_wv_m": np.concatenate([np.ascontiguousarray(Wvf.T),
                                   bvf.reshape(1, 512)], axis=0),
        "l0_wo_t": _tiles_T(np.asarray(d["L0_attn_Wo_w"], np.float32),
                            np.asarray(d["L0_attn_Wo_b"], np.float32)),
        "f0_w_t": _tiles_T(np.asarray(d["L0_ffn_W_base"], np.float32),
                           np.asarray(d["L0_ffn_b_base"], np.float32)),
        "f0_a_t": np.ascontiguousarray(A0.T.reshape(1, D, 32)),
        "f0_b_t": np.ascontiguousarray(B0s.reshape(NCH, 128, 32).transpose(0, 2, 1)),
        "f0_r_t": np.ascontiguousarray(rW.T.reshape(1, D, 4)),
        "f0_rb": np.asarray(d["L0_ffn_router_b"], np.float32)[act].reshape(1, 4),
        "f0_prior": np.asarray(d["L0_ffn_domain_prior"], np.float32)[act].reshape(1, 4),
        "l1_wq_t": _tiles_T(wqkv[:1024]),
        "l1_wk_t": _tiles_T(wqkv[1024:1280]),
        "l1_wvc_m": np.ascontiguousarray(wqkv[1280:1408].T),
        "l1_wvp_m": np.ascontiguousarray(wqkv[1408:1536].T),
        "qc1_s": _conv1_slab(np.asarray(d["L1_attn_qc1_w"], np.float32),
                             np.asarray(d["L1_attn_qc1_b"], np.float32)),
        "qc2_s": _conv2_slab(np.asarray(d["L1_attn_qc2_w"], np.float32),
                             np.asarray(d["L1_attn_qc2_b"], np.float32)),
        "kc1_s": _conv1_slab(np.asarray(d["L1_attn_kc1_w"], np.float32),
                             np.asarray(d["L1_attn_kc1_b"], np.float32)),
        "kc2_s": _conv2_slab(np.asarray(d["L1_attn_kc2_w"], np.float32),
                             np.asarray(d["L1_attn_kc2_b"], np.float32)),
        "l1_wo_t": _tiles_T(np.asarray(d["L1_attn_w_o"], np.float32)),
        "f1_w_t": _tiles_T(np.asarray(d["L1_ffn_W"], np.float32),
                           np.asarray(d["L1_ffn_b"], np.float32)),
        "f1_a_t": np.ascontiguousarray(A1s.T.reshape(1, D, 128)),
        "f1_b_t": np.ascontiguousarray(B1.reshape(P, NCH, 128).transpose(1, 0, 2)),
    }

    x_full = np.asarray(d["L0_attn_x"], np.float32)
    key_temp = float(np.asarray(d["L1_attn_key_temp"]).reshape(-1)[0])
    p_scores = np.asarray(d["L1_ffn_p_scores"], np.float32)

    cpack = np.zeros((128, 512), np.float32)
    for pp in range(128):
        cpack[pp, pp % 64] = 1.0 / 16.0
        cpack[pp, 64 + pp % 64] = 1.0 / 4.0
        cpack[pp, 256 + pp // 64] = 1.0
    for dd in range(64):
        cpack[dd, 128 + dd] = 0.5
        cpack[dd, 192 + dd] = 0.5
    for e in range(2):
        cpack[e, 258 + e * 64: 258 + (e + 1) * 64] = 1.0
    for e in range(4):
        cpack[e, 386 + e * 8: 386 + (e + 1) * 8] = 2.0
    cpack[:, 128:256] = 0.0
    for dd in range(64):
        cpack[dd, 128 + dd] = 0.5
        cpack[dd, 128 + 64 + dd] = 0.5

    eye = np.eye(128, dtype=np.float32)
    inv = 1.0 / 10000.0 ** (np.arange(0, 64, 2, dtype=np.float32) / 64.0)

    in_maps = []
    for c in range(8):
        b, hf = c // 2, c % 2
        off = hf * T
        pos = np.arange(T, dtype=np.float32) + off
        ang = pos[None, :] * inv[:, None]
        ctab = np.zeros((128, T), np.float32)
        stab = np.zeros((128, T), np.float32)
        for pp in range(128):
            dd = pp % 64
            ctab[pp] = np.cos(ang[dd // 2])
            stab[pp] = np.sin(ang[dd // 2]) * (1.0 if dd % 2 == 1 else -1.0)
        jj = np.arange(1920)
        kk_ = np.arange(128)
        strip = (kk_[:, None] <= (off + jj[None, :] - 1408)).astype(np.float32)
        keybias = np.zeros((1, 2048), np.float32)
        if hf == 0:
            keybias[0, 1024:] = -16384.0
        consts = np.zeros((128, 16), np.float32)
        consts[:, 0] = 1e-5
        consts[:, 1] = -0.5
        consts[:, 2] = 0.125
        consts[:, 3] = float(hf)
        consts[:, 4] = 1.0 / D
        consts[:, 5] = 1e-12
        consts[:, 6] = 1e-6
        consts[:, 7] = 0.5
        consts[:, 8] = key_temp
        consts[:, 9] = 1.0
        m = {
            "xin": np.ascontiguousarray(x_full[b, off:off + T].T),
            "consts": consts,
            "cpack": cpack.astype(BFT),
            "eyeb": eye.astype(BFT),
            "eyef": eye,
            "ropec": ctab.astype(BFT),
            "ropes": stab.astype(BFT),
            "strip": strip.astype(BFT),
            "keybias": keybias.astype(BFT),
            "p_rep": np.ascontiguousarray(np.repeat(p_scores, 16)).reshape(P, 1),
        }
        m.update(shared)
        in_maps.append(m)
    return in_maps



class SemC:
    def __init__(self, handle):
        self.h = handle
        self.n = 0

    def tick(self, step=1):
        self.n += step
        return (self.h, step)


class Op:
    __slots__ = ("method", "args", "kw", "incs")

    def __init__(self, method, args, kw, inc):
        self.method = method
        self.args = args
        self.kw = kw
        self.incs = [inc] if inc is not None else []

    def run(self, h):
        try:
            r = getattr(h, self.method)(*self.args, **self.kw)
        except Exception:
            print("EMIT FAIL:", self.method,
                  [getattr(x, "shape", x) for x in self.args])
            raise
        for (sh, st) in self.incs:
            r.then_inc(sh, st)


class Wt:
    __slots__ = ("sem", "val")

    def __init__(self, sem, val):
        self.sem = sem
        self.val = val

    def run(self, h):
        h.wait_ge(self.sem, self.val)


class Prog:
    def __init__(self):
        self.q = {e: [] for e in ENGINES}

    def emit(self, eng, method, *args, inc=None, **kw):
        self.q[eng].append(Op(method, args, kw, inc))

    def tick_last(self, eng, semtick):
        qq = self.q[eng]
        i = len(qq) - 1
        while i >= 0 and not isinstance(qq[i], Op):
            i -= 1
        assert i >= 0 and len(qq[i].incs) == 0, f"tick_last conflict {qq[i].method} {qq[i].incs}"
        qq[i].incs.append(semtick)

    def wait(self, eng, sem, val=None):
        v = sem.n if val is None else val
        if v <= 0:
            return
        self.q[eng].append(Wt(sem.h, v))


class Arena:
    def __init__(self, nc, es, name, size):
        self.nc = nc
        self.name = name
        self.size = size
        t = es.enter_context(nc.sbuf_tensor(name, [P, size], U8))
        self.base = nc.lookup_mloc(t).addr
        self.ctr = 0

    def at(self, offset, shape, dtype, name=""):
        nbytes = int(np.prod(shape[1:])) * mybir.dt.size(dtype)
        assert offset + nbytes <= self.size, (self.name, name, offset, nbytes)
        self.ctr += 1
        nm = f"{self.name}{self.ctr}_{name}"
        h = self.nc.virtual_sbuf_tensor(nm, list(shape), dtype)
        self.nc._tensor(h.name, list(h.shape), dtype, type="SB")
        mloc = self.nc.lookup_mloc(h)
        mloc.allocated = True
        mloc.addr = self.base + offset
        return h


def build_program(dumps=(), nphase=99):
    nc = bass.Bass("TRN2", target_bir_lowering=False,
                   detect_race_conditions=False)
    DT = {}

    def din(name, shape, dt=F32):
        DT[name] = nc.dram_tensor(name, list(shape), dt, kind="ExternalInput")

    din("xin", (D, T)); din("consts", (P, 16)); din("cpack", (P, 512), BF16)
    din("eyeb", (P, P), BF16); din("eyef", (P, P))
    din("ropec", (P, T), BF16); din("ropes", (P, T), BF16)
    din("strip", (P, 1920), BF16); din("keybias", (1, 2048), BF16)
    din("p_rep", (P, 1))
    din("l0_wq_t", (4, D + 1, 128)); din("l0_wk_t", (4, D + 1, 128))
    din("l0_wv_m", (D + 1, 512)); din("l0_wo_t", (32, 513, 128))
    din("f0_w_t", (32, D + 1, 128)); din("f0_a_t", (1, D, 32))
    din("f0_b_t", (32, 32, 128)); din("f0_r_t", (1, D, 4))
    din("f0_rb", (1, 4)); din("f0_prior", (1, 4))
    din("l1_wq_t", (8, D, 128)); din("l1_wk_t", (2, D, 128))
    din("l1_wvc_m", (D, 128)); din("l1_wvp_m", (D, 128))
    din("qc1_s", (8, 385, 128)); din("qc2_s", (8, 3073, 128))
    din("kc1_s", (2, 385, 128)); din("kc2_s", (2, 769, 128))
    din("l1_wo_t", (32, 1024, 128))
    din("f1_w_t", (32, D + 1, 128)); din("f1_a_t", (1, D, 128))
    din("f1_b_t", (32, 128, 128))

    yout = nc.dram_tensor("yout", [D, T], F32, kind="ExternalOutput")
    xsp = nc.dram_tensor("xsp", [NCH - NRES, P, T], F32)
    X1 = 512 * T + T * 512
    X2 = 8 * 128 * 4 + 2 * 128 * 4 + 128
    X3 = 256 * T + T * 256
    nmd = nc.dram_tensor("nmd", [8, 128], BF16)
    ccin = {1: nc.dram_tensor("cc1i", [1, X1], BF16),
            2: nc.dram_tensor("cc2i", [1, X2], BF16),
            3: nc.dram_tensor("cc3i", [1, X3], BF16)}
    ccout = {1: nc.dram_tensor("cc1o", [2, X1], BF16),
             2: nc.dram_tensor("cc2o", [2, X2], BF16),
             3: nc.dram_tensor("cc3o", [2, X3], BF16)}
    dbg_shapes = {"ln": (P, NCH * 1025), "q0": (P, 4096), "k0": (P, 4096),
                  "v0": (P, 4096), "x": (D, T), "at0": (P, 4096),
                  "qpre": (P, 8 * 1028), "kpre": (P, 2 * 1028),
                  "v1own": (P, 2048), "qc1o": (P, 8 * 1026), "qfin": (P, 8192),
                  "kfin": (P, 2048), "at1": (P, 8192), "alfm": (4, T),
                  "u1": (P, T), "lgfm": (4, T), "kh0": (P, S), "vh0": (P, 1040),
                  "nmr0": (1, T), "kg0": (P, S), "vg0": (P, 1040),
                  "nmx": (P, 8), "lgtm": (P, 32), "altm": (P, 32)}
    dbg = {}
    for k in dumps:
        dbg[k] = nc.dram_tensor("dbg_" + k, list(dbg_shapes[k]), F32,
                                kind="ExternalOutput")

    es = ExitStack()
    with es:
        XR = es.enter_context(nc.sbuf_tensor("XR", [P, NRES * T * 4], U8))
        xr_base = nc.lookup_mloc(XR).addr
        xres = nc.virtual_sbuf_tensor("xresv", [P, NRES * T], F32)
        nc._tensor(xres.name, list(xres.shape), F32, type="SB")
        _m = nc.lookup_mloc(xres)
        _m.allocated = True
        _m.addr = xr_base

        BIG = Arena(nc, es, "BG", 73824)
        MID = Arena(nc, es, "MD", 37440)
        SMF = es.enter_context(nc.sbuf_tensor("SMF", [P, 3136], F32))
        SMB = es.enter_context(nc.sbuf_tensor("SMB", [P, 5632], BF16))
        CST = es.enter_context(nc.sbuf_tensor("CST", [P, 16], F32))
        EYE = es.enter_context(nc.sbuf_tensor("EYE", [P, P], BF16))
        EYEF = es.enter_context(nc.sbuf_tensor("EYEF", [P, P], F32))
        CPK = es.enter_context(nc.sbuf_tensor("CPK", [P, 512], BF16))
        RPC = es.enter_context(nc.sbuf_tensor("RPC", [P, T], BF16))
        RPS = es.enter_context(nc.sbuf_tensor("RPS", [P, T], BF16))
        STRP = es.enter_context(nc.sbuf_tensor("STRP", [P, 1920], BF16))
        PRP = es.enter_context(nc.sbuf_tensor("PRP", [P, 1], F32))

        ln = BIG.at(0, (P, NCH * 1025), BF16, "ln")
        q0 = MID.at(0, (P, 4 * T), BF16, "q0")
        k_own = MID.at(8192, (P, 4 * T), BF16, "kown")
        wsl = [MID.at(16448 + i * 4352, (P, 2176), BF16, f"w{i}") for i in range(2)]
        rot = [MID.at(25152 + i * 2048, (P, T), BF16, f"r{i}") for i in range(2)]
        a_bc = MID.at(29248, (P, T), BF16, "abc")
        ma_bc = MID.at(31296, (P, T), BF16, "mabc")
        xstg = MID.at(33344, (P, T), F32, "xstg")
        v_own = BIG.at(65600, (P, 8 * 512), BF16, "vown")
        at0 = BIG.at(0, (P, 4 * T), BF16, "at0")
        p0 = [BIG.at(8192 + i * 2048, (P, T), BF16, f"p0{i}") for i in range(2)]
        kh = [BIG.at(12288 + i * 4096, (P, S), BF16, f"kh{i}") for i in range(2)]
        vh = [BIG.at(20480 + i * 4160, (P, 16 * 65), BF16, f"vh{i}") for i in range(2)]
        nmx = BIG.at(28800, (P, 8), F32, "nmx")
        nmxb = BIG.at(28864, (P, 8), BF16, "nmxb")
        nmst = BIG.at(28896, (8, P), BF16, "nmst")
        nmrow = [BIG.at(29184 + i * 2048, (1, T), BF16, f"nmr{i}") for i in range(2)]
        rbuf = BIG.at(33280, (64, 512), BF16, "rbuf")
        a0sb = MID.at(0, (P, 1024), BF16, "a0sb")
        ballsb = MID.at(2048, (32, 4096), BF16, "ball")
        rtw = MID.at(10240, (P, 128), BF16, "rtw")
        lgfm = MID.at(10496, (4, T), F32, "lgfm")
        lgtm = MID.at(14592, (P, 32), F32, "lgtm")
        altm = MID.at(14720, (P, 32), F32, "altm")
        alfm = BIG.at(69696, (4, T), BF16, "alfm")
        zl = BIG.at(65600, (32, T), BF16, "zl")
        absb = BIG.at(67648, (32, T), BF16, "absb")
        qpre = MID.at(0, (P, 8 * 1028), BF16, "qpre")
        kpre = BIG.at(69696, (P, 2 * 1028), BF16, "kpre")
        wslc = [MID.at(20560 + i * 6400, (P, 25 * 128), BF16, f"wc{i}") for i in range(2)]
        v1own = BIG.at(65600, (P, 8 * 256), BF16, "v1own")
        qc1o = BIG.at(0, (P, 8 * 1026), BF16, "qc1o")
        qfin = BIG.at(16416, (P, 8 * T), BF16, "qfin")
        kfin = BIG.at(32800, (P, 2 * T), BF16, "kfin")
        kc1o = BIG.at(36896, (P, 2 * 1026), BF16, "kc1o")
        kmb = BIG.at(41024, (64, T), BF16, "kmb")
        qmb = BIG.at(43072, (64, T), BF16, "qmb")
        rstb = BIG.at(45120, (P, T), BF16, "rstb")
        qhat = BIG.at(47168, (P, T), BF16, "qhat")
        qswp = BIG.at(49216, (P, T), BF16, "qswp")
        qt1 = BIG.at(51264, (P, T), BF16, "qt1")
        q2r = BIG.at(53312, (P, T), BF16, "q2r")
        hstg = BIG.at(55360, (P, 192), BF16, "hstg")
        at1 = BIG.at(0, (P, 8 * T), BF16, "at1")
        kg = [BIG.at(32800 + i * 4096, (P, S), BF16, f"kg{i}") for i in range(2)]
        vg = [BIG.at(41024 + i * 4160, (P, 16 * 65), BF16, f"vg{i}") for i in range(2)]
        p1b = [BIG.at(49408 + i * 2048, (P, T), BF16, f"p1{i}") for i in range(2)]
        pm1 = [BIG.at(53504 + i * 2048, (P, T), BF16, f"pm{i}") for i in range(2)]
        rb1 = BIG.at(57600, (64, 512), BF16, "rb1")
        u1 = BIG.at(65600, (P, T), BF16, "u1")
        gm_bc = BIG.at(8192, (P, T), BF16, "gmbc")
        tfr = BIG.at(12288, (P, T), F32, "tfr")

        psA = es.enter_context(nc.psum_tensor("psA", [P, S], F32))
        psB = es.enter_context(nc.psum_tensor("psB", [P, T], F32))
        psC = es.enter_context(nc.psum_tensor("psC", [P, T], F32))

        sL = SemC(es.enter_context(nc.semaphore("sL")))
        sG = SemC(es.enter_context(nc.semaphore("sG")))
        sT = SemC(es.enter_context(nc.semaphore("sT")))
        sV = SemC(es.enter_context(nc.semaphore("sV")))
        sS = SemC(es.enter_context(nc.semaphore("sS")))
        sO = SemC(es.enter_context(nc.semaphore("sO")))
        sBa = SemC(es.enter_context(nc.semaphore("sBa")))
        sC = SemC(es.enter_context(nc.semaphore("sC")))

        p = Prog()
        emit, wait = p.emit, p.wait

        def barrier():
            emit("tensor", "matmul", psB[0:1, 1020:1021], ONES_ROW[0:1, 0:1],
                 ONES_ROW[0:1, 0:1], start=True, stop=True, inc=sBa.tick())
            emit("vector", "tensor_copy", SMF[0:1, 3132:3133],
                 SMF[0:1, 3130:3131], inc=sBa.tick())
            emit("scalar", "copy", SMF[0:1, 3133:3134], SMF[0:1, 3131:3132],
                 inc=sBa.tick())
            emit("gpsimd", "memset", SMF[0:1, 3130:3131], 1.0, inc=sBa.tick())
            tgt = sBa.n
            for e in ENGINES:
                p.wait(e, sBa, tgt)

        def ln_ap(c, t0=0, tn=T):
            return ln[:, c * 1025 + 1 + t0: c * 1025 + 1 + t0 + tn]

        state = {"xloc": "xin"}

        def xsp_read_ap(c):
            if state["xloc"] == "xin":
                return DT["xin"][c * 128:(c + 1) * 128, :]
            return xsp[c - NRES, :, :]

        ONES_ROW = SMB[0:1, 0:512]
        ONES_COL = SMB[:, 0:1]
        SEL16 = CPK[:, 0:64]
        SEL4 = CPK[:, 64:128]
        DUP64 = CPK[0:64, 128:256]
        SEL2H = CPK[:, 256:258]
        BAND2 = CPK[0:2, 258:386]
        SEL2E = CPK[0:4, 386:418]
        KB_ROW = SMB[0:1, 512:2560]
        ABF = SMB[0:1, 3000:4024]
        MABF = SMB[0:1, 4096:5120]
        RBF = SMB[0:1, 5120:5632]
        RSB2 = SMB[0:2, 2560:2560 + 1024]
        GBF = SMB[0:1, 4096:5120]
        VB_ROW = SMB[0:1, 2560:3072]
        CBR = SMB[0:1, 2560:2564]

        for nm_, dst in (("consts", CST), ("cpack", CPK), ("eyeb", EYE),
                         ("eyef", EYEF), ("ropec", RPC), ("ropes", RPS),
                         ("strip", STRP), ("p_rep", PRP)):
            emit("sync", "dma_start", out=dst[:, :], in_=DT[nm_][:, :], inc=sL.tick(16))
        emit("sync", "dma_start", out=KB_ROW, in_=DT["keybias"][:, :], inc=sL.tick(16))
        for c in range(NRES):
            emit("sync", "dma_start", out=xres[:, c * T:(c + 1) * T],
                 in_=DT["xin"][c * 128:(c + 1) * 128, :], inc=sL.tick(16))
        emit("vector", "memset", SMB[:, 0:512], 1.0)
        emit("vector", "memset",
             bass.AP(ln, 0, [[NCH * 1025, P], [1025, NCH], [1, 1]]), 0.0)
        p.tick_last("vector", sV.tick())
        ini_l, ini_v = sL.n, sV.n
        for e in ENGINES:
            wait(e, sL, ini_l)
            wait(e, sV, ini_v)

        def ln_phase():
            for c in range(NCH):
                if c < NRES:
                    src = xres[:, c * T:(c + 1) * T]
                else:
                    wait("sync", sS, sS.n)
                    wait("sync", sV, sV.n)
                    emit("sync", "dma_start", out=xstg[:, :], in_=xsp_read_ap(c),
                         inc=sL.tick(16))
                    wait("scalar", sL, sL.n)
                    wait("vector", sL, sL.n)
                    src = xstg[:, :]
                r = rot[c % 2]
                if c >= 2:
                    wait("vector", sT, sT.n)
                emit("vector", "tensor_mul", r[:, :], src, src, inc=sV.tick())
                emit("scalar", "copy", ln_ap(c), src, inc=sS.tick())
                wait("tensor", sS, sS.n)
                wait("tensor", sV, sV.n)
                for t2 in range(2):
                    emit("tensor", "matmul", psC[0:1, t2 * 512:(t2 + 1) * 512],
                         ONES_COL, ln_ap(c, t2 * 512, 512),
                         start=(c == 0), stop=(c == NCH - 1))
                    emit("tensor", "matmul", psC[32:33, t2 * 512:(t2 + 1) * 512],
                         ONES_COL, r[:, t2 * 512:(t2 + 1) * 512],
                         start=(c == 0), stop=(c == NCH - 1))
                p.tick_last("tensor", sT.tick())
            wait("vector", sT, sT.n)
            MEAN = SMF[0:1, 0:1024]
            V1 = SMF[0:1, 1024:2048]
            V2 = SMF[0:1, 2048:3072]
            emit("vector", "tensor_scalar_mul", MEAN, psC[0:1, 0:T], 1.0 / D)
            emit("vector", "tensor_scalar_mul", V1, psC[32:33, 0:T], 1.0 / D)
            emit("vector", "tensor_mul", V2, MEAN, MEAN)
            emit("vector", "tensor_sub", V1, V1, V2, inc=sV.tick())
            wait("scalar", sV, sV.n)
            emit("scalar", "activation", V2, V1, AF.Ln, bias=CST[0:1, 0:1])
            emit("scalar", "activation", V2, V2, AF.Exp,
                 scale=CST[0:1, 1:2], inc=sS.tick())
            wait("vector", sS, sS.n)
            emit("vector", "tensor_mul", V1, MEAN, V2)
            emit("vector", "tensor_copy", ABF, V2)
            emit("vector", "tensor_copy", MABF, V1, inc=sV.tick())
            wait("tensor", sV, sV.n)
            for t2 in range(2):
                emit("tensor", "matmul", psB[:, t2 * 512:(t2 + 1) * 512],
                     ONES_ROW[0:1, 0:128], ABF[0:1, t2 * 512:(t2 + 1) * 512],
                     start=True, stop=True)
                emit("tensor", "matmul", psA[:, t2 * 512:(t2 + 1) * 512],
                     ONES_ROW[0:1, 0:128], MABF[0:1, t2 * 512:(t2 + 1) * 512],
                     start=True, stop=True)
            p.tick_last("tensor", sT.tick())
            wait("vector", sT, sT.n)
            emit("vector", "tensor_copy", a_bc[:, :], psB[:, 0:T])
            emit("vector", "tensor_copy", ma_bc[:, :], psA[:, 0:T], inc=sV.tick())
            for c in range(NCH):
                r = rot[c % 2]
                emit("vector", "tensor_mul", r[:, :], ln_ap(c), a_bc[:, :])
                emit("vector", "tensor_sub", ln_ap(c), r[:, :], ma_bc[:, :])
            p.tick_last("vector", sV.tick())
            barrier()

        def gemm_fm(wt, nm, nk, evac, moving=None, bias=True, mw=128, extra=None):
            moving = moving or (lambda kc, t0, tn: ln_ap(kc, t0, tn))
            rows = nk * 128 + (1 if bias else 0)
            nsl = 2 if rows > 2048 else 1
            slabs = []
            for m in range(nm):
                if nsl == 1:
                    slabs.append((m, 0, 0, rows))
                else:
                    slabs.append((m, 0, 0, 2048))
                    slabs.append((m, 1, 2048, rows))
            sg_at = [0] * len(slabs)
            free_tok = [0] * len(slabs)
            vdone = {}
            sdone = {}
            ent_v, ent_s = sV.n, sS.n
            ent_t = sT.n

            def load(i):
                m, s2, r0, r1 = slabs[i]
                bi = i % 2
                if i >= 2:
                    wait("gpsimd", sT, free_tok[i - 2])
                else:
                    wait("gpsimd", sT, ent_t)
                kdim = r1 - r0
                nfull = kdim // 128
                if nfull:
                    R_ = wt.shape[1]
                    emit("gpsimd", "dma_start",
                         out=bass.AP(wsl[bi], 0,
                                     [[2176, 128], [mw, nfull], [1, mw]]),
                         in_=bass.AP(wt, (m * R_ + r0) * mw,
                                     [[mw, 128], [128 * mw, nfull], [1, mw]]),
                         inc=sG.tick(16))
                if kdim % 128:
                    emit("gpsimd", "dma_start",
                         out=wsl[bi][0:1, nfull * mw:nfull * mw + mw],
                         in_=wt[m, r0 + nfull * 128:r0 + nfull * 128 + 1, :],
                         inc=sG.tick(16))
                sg_at[i] = sG.n

            def compute(i):
                m, s2, r0, r1 = slabs[i]
                bi = i % 2
                slot = m % 2
                pa = psA[0:mw, slot * T:slot * T + T]
                wait("tensor", sG, sg_at[i])
                if s2 == 0 and m >= 2:
                    wait("tensor", sV, vdone[m - 2])
                    wait("tensor", sS, sdone[m - 2])
                elif s2 == 0:
                    wait("tensor", sV, ent_v)
                    wait("tensor", sS, ent_s)
                kdim = r1 - r0
                nfull = kdim // 128
                last = (s2 == nsl - 1)
                for kc in range(nfull):
                    kglob = (r0 // 128) + kc
                    for t2 in range(2):
                        emit("tensor", "matmul", pa[:, t2 * 512:(t2 + 1) * 512],
                             wsl[bi][:, kc * mw:(kc + 1) * mw],
                             moving(kglob, t2 * 512, 512),
                             start=(s2 == 0 and kc == 0), stop=False)
                if kdim % 128 and last:
                    for t2 in range(2):
                        emit("tensor", "matmul", pa[:, t2 * 512:(t2 + 1) * 512],
                             wsl[bi][0:1, nfull * mw:nfull * mw + mw],
                             ONES_ROW[0:1, 0:512],
                             start=False, stop=(extra is None and t2 == 1))
                if last and extra is not None:
                    extra(m, pa)
                p.tick_last("tensor", sT.tick())
                free_tok[i] = sT.n
                if last:
                    evac(m, pa)
                    vdone[m] = sV.n
                    sdone[m] = sS.n

            load(0)
            for i in range(1, len(slabs)):
                load(i)
                compute(i - 1)
            compute(len(slabs) - 1)

        def gemm_tm(wm, O, out_fn, bias_row=None, shift=0, pslots=None,
                    extra_tok=None, extra_ps=None):
            kcg = 2048 // O
            nslab = NCH // kcg
            sg_at = [0] * nslab
            free_tok = [0] * nslab
            ent_v2, ent_s2 = sV.n, sS.n
            ent_t2 = sT.n

            def load(i):
                bi = i % 2
                if i >= 2:
                    wait("gpsimd", sT, free_tok[i - 2])
                else:
                    wait("gpsimd", sT, ent_t2)
                r0 = i * kcg * 128
                emit("gpsimd", "dma_start",
                     out=bass.AP(wsl[bi], 0, [[2176, 128], [O, kcg], [1, O]]),
                     in_=bass.AP(wm, r0 * O, [[O, 128], [128 * O, kcg], [1, O]]),
                     inc=sG.tick(16))
                sg_at[i] = sG.n

            def compute(i):
                bi = i % 2
                wait("tensor", sG, sg_at[i])
                if i == 0:
                    wait("tensor", sV, ent_v2)
                    wait("tensor", sS, ent_s2)
                for kt in range(8):
                    for kci in range(kcg):
                        kglob = i * kcg + kci
                        lhs = ln[:, kglob * 1025 + 1 + kt * 128 - shift:
                                 kglob * 1025 + 1 + kt * 128 - shift + 128]
                        emit("tensor", "matmul", pslots[kt],
                             lhs, wsl[bi][:, kci * O:(kci + 1) * O],
                             start=(kglob == 0 and kt % (512 // O) == 0),
                             stop=(kglob == NCH - 1 and bias_row is None))
                if extra_tok is not None:
                    for kci in range(kcg):
                        kglob = i * kcg + kci
                        lhs = ln[:, kglob * 1025 + 1 + extra_tok:
                                 kglob * 1025 + 1 + extra_tok + 1]
                        emit("tensor", "matmul", extra_ps,
                             lhs, wsl[bi][:, kci * O:(kci + 1) * O],
                             start=(kglob == 0), stop=(kglob == NCH - 1))
                p.tick_last("tensor", sT.tick())
                free_tok[i] = sT.n

            load(0)
            for i in range(1, nslab):
                load(i)
                compute(i - 1)
            compute(nslab - 1)
            if bias_row is not None:
                for kt in range(8):
                    emit("tensor", "matmul", pslots[kt], ONES_ROW[0:1, 0:128],
                         bias_row[0:1, 0:O], start=False, stop=True)
                p.tick_last("tensor", sT.tick())
            wait("vector", sT, sT.n)
            for kt in range(8):
                out_fn(kt, pslots[kt])
            p.tick_last("vector", sV.tick())

        def ev_resid(m, pa, store=False):
            wait("vector", sT, sT.n)
            if m < NRES:
                emit("vector", "tensor_add", xres[:, m * T:(m + 1) * T],
                     xres[:, m * T:(m + 1) * T], pa[:, 0:T], inc=sV.tick())
                if store:
                    wait("sync", sV, sV.n)
                    emit("sync", "dma_start", out=yout[m * 128:(m + 1) * 128, :],
                         in_=xres[:, m * T:(m + 1) * T], inc=sO.tick(16))
            else:
                wait("sync", sV, sV.n)
                emit("sync", "dma_start", out=xstg[:, :], in_=xsp_read_ap(m),
                     inc=sL.tick(16))
                wait("vector", sL, sL.n)
                emit("vector", "tensor_add", xstg[:, :], xstg[:, :], pa[:, 0:T],
                     inc=sV.tick())
                wait("sync", sV, sV.n)
                emit("sync", "dma_start", out=xsp[m - NRES, :, :], in_=xstg[:, :],
                     inc=sL.tick(16))
                if store:
                    emit("sync", "dma_start", out=yout[m * 128:(m + 1) * 128, :],
                         in_=xstg[:, :], inc=sO.tick(16))

        def dump_x(key):
            if key in dbg:
                barrier()
                for m in range(NCH):
                    if m < NRES:
                        emit("sync", "dma_start",
                             out=dbg[key][m * 128:(m + 1) * 128, :],
                             in_=xres[:, m * T:(m + 1) * T], inc=sO.tick(16))
                    else:
                        wait("sync", sO, sO.n)
                        emit("sync", "dma_start", out=xstg[:, :],
                             in_=xsp_read_ap(m), inc=sL.tick(16))
                        wait("sync", sL, sL.n)
                        emit("sync", "dma_start",
                             out=dbg[key][m * 128:(m + 1) * 128, :],
                             in_=xstg[:, :], inc=sO.tick(16))
                barrier()

        def dump_bf(key, buf):
            if key in dbg:
                barrier()
                rows, cols = dbg_shapes[key]
                emit("gpsimd", "dma_start", out=dbg[key][:, :],
                     in_=buf[0:rows, 0:cols], inc=sO.tick(16))
                barrier()

        ln_phase()
        dump_bf("ln", ln)

        if nphase >= 1:
            def ev_q0(m, pa):
                wait("vector", sT, sT.n)
                emit("vector", "tensor_copy", q0[:, m * T:(m + 1) * T],
                     pa[:, 0:T], inc=sV.tick())

            def ev_k0(m, pa):
                wait("vector", sT, sT.n)
                emit("vector", "tensor_copy", k_own[:, m * T:(m + 1) * T],
                     pa[:, 0:T], inc=sV.tick())

            gemm_fm(DT["l0_wq_t"], 4, 32, ev_q0)
            gemm_fm(DT["l0_wk_t"], 4, 32, ev_k0)
            emit("gpsimd", "dma_start", out=VB_ROW,
                 in_=DT["l0_wv_m"][D:D + 1, :], inc=sG.tick(16))
            wait("tensor", sG, sG.n)
            vps = [psA[:, i * 512:(i + 1) * 512] for i in range(4)] + \
                  [psB[:, i * 512:(i + 1) * 512] for i in range(2)] + \
                  [psC[:, i * 512:(i + 1) * 512] for i in range(2)]

            def ev_v0(kt, ps):
                emit("vector", "tensor_copy", v_own[:, kt * 512:(kt + 1) * 512],
                     ps)

            gemm_tm(DT["l0_wv_m"], 512, ev_v0, bias_row=VB_ROW, pslots=vps)
            wait("sync", sV, sV.n)
            emit("sync", "dma_start",
                 out=bass.AP(ccin[1], 0, [[4096, 128], [1, 4096]]),
                 in_=k_own[:, :], inc=sL.tick(16))
            emit("sync", "dma_start",
                 out=bass.AP(ccin[1], 512 * T, [[4096, 128], [1, 4096]]),
                 in_=v_own[:, :], inc=sL.tick(16))
            wait("gpsimd", sL, sL.n)
            emit("gpsimd", "collective_compute", "AllGather", ALU.bypass,
                 RG, [ccin[1].ap().opt()], [ccout[1].ap().opt()], inc=sC.tick())
            dump_bf("q0", q0)
            dump_bf("k0", k_own)
            dump_bf("v0", v_own)

        if nphase >= 2:
            wait("sync", sC, sC.n)
            for vv in (vh[0], vh[1]):
                emit("vector", "memset",
                     bass.AP(vv, 64, [[16 * 65, P], [65, 16], [1, 1]]), 1.0)
            emit("vector", "tensor_copy", SMF[0:1, 3132:3133],
                 SMF[0:1, 3130:3131], inc=sV.tick())
            kv_at = {}
            done_tok = {}

            def l0_load(h):
                if h >= 2:
                    wait("sync", sT, done_tok[h - 2])
                for s in range(2):
                    for par2 in range(2):
                        emit("sync", "dma_start",
                             out=kh[h % 2][par2 * 64:(par2 + 1) * 64,
                                           s * 1024:(s + 1) * 1024],
                             in_=bass.AP(ccout[1], s * X1 +
                                         ((h % 2) * 64) * 4096 + (h // 2) * 1024,
                                         [[4096, 64], [1, 1024]]),
                             inc=sL.tick(16))
                    emit("sync", "dma_start",
                         out=bass.AP(vh[h % 2], s * 8 * 65,
                                     [[16 * 65, 128], [65, 8], [1, 64]]),
                         in_=bass.AP(ccout[1], s * X1 + 512 * T + h * 64,
                                     [[4096, 128], [512, 8], [1, 64]]),
                         inc=sL.tick(16))
                kv_at[h] = sL.n

            l0_load(0)
            prev_norm_v = 0
            l0_exp_last = {0: 0, 1: 0}
            for h in range(8):
                par, hp = h % 2, h // 2
                if h + 1 < 8:
                    l0_load(h + 1)
                wait("tensor", sL, kv_at[h])
                vred = {}
                for qt in range(8):
                    pg = qt % 2
                    if pg == 0:
                        tiles = (psA[:, 0:1024], psA[:, 1024:2048])
                    else:
                        tiles = (psB[:, 0:1024], psC[:, 0:1024])
                    if qt >= 2:
                        wait("tensor", sV, vred[qt - 2])
                    else:
                        wait("tensor", sV, sV.n)
                    for kc4 in range(4):
                        emit("tensor", "matmul",
                             tiles[kc4 // 2][:, (kc4 % 2) * 512:(kc4 % 2) * 512 + 512],
                             q0[par * 64:(par + 1) * 64,
                                hp * T + qt * 128:hp * T + qt * 128 + 128],
                             kh[h % 2][par * 64:(par + 1) * 64,
                                       kc4 * 512:(kc4 + 1) * 512],
                             start=True, stop=True)
                    p.tick_last("tensor", sT.tick())
                    wait("vector", sT, sT.n)
                    t0c = 3120 + pg * 2
                    emit("vector", "reduce_max", SMF[:, t0c:t0c + 1], tiles[0],
                         mybir.AxisListType.X)
                    emit("vector", "reduce_max", SMF[:, t0c + 1:t0c + 2], tiles[1],
                         mybir.AxisListType.X)
                    emit("vector", "tensor_max", SMF[:, t0c:t0c + 1],
                         SMF[:, t0c:t0c + 1], SMF[:, t0c + 1:t0c + 2])
                    emit("vector", "tensor_scalar_mul", nmx[:, qt:qt + 1],
                         SMF[:, t0c:t0c + 1], -1.0, inc=sV.tick())
                    vred[qt] = sV.n
                emit("vector", "tensor_copy", nmxb[:, :], nmx[:, :], inc=sV.tick())
                if h == 6 and "nmx" in dbg:
                    wait("sync", sV, sV.n)
                    emit("sync", "dma_start", out=dbg["nmx"][:, :], in_=nmx[:, :],
                         inc=sO.tick(16))
                wait("tensor", sV, sV.n)
                for qt in range(8):
                    emit("tensor", "matmul", psB[0:1, qt * 128:(qt + 1) * 128],
                         nmx[:, qt:qt + 1], EYEF[:, :], start=True, stop=True)
                p.tick_last("tensor", sT.tick())
                wait("vector", sT, sT.n)
                emit("vector", "tensor_copy", nmrow[h % 2][0:1, :],
                     psB[0:1, 0:T], inc=sV.tick())
                nm_v = sV.n
                wait("tensor", sV, nm_v)
                for cq in range(2):
                    def l0_av(ping_, kbp_, exp_tick):
                        wait("tensor", sS, exp_tick)
                        if kbp_ == 0:
                            wait("tensor", sV, prev_norm_v)
                        for j in range(2):
                            emit("tensor", "matmul",
                                 psC[0:65, cq * 512:(cq + 1) * 512],
                                 vh[h % 2][:, (2 * kbp_ + j) * 65:
                                           (2 * kbp_ + j + 1) * 65],
                                 p0[ping_][:, j * 512:(j + 1) * 512],
                                 start=(kbp_ == 0 and j == 0),
                                 stop=(kbp_ == 7 and j == 1))

                    pend = None
                    for kbp in range(8):
                        ping = kbp % 2
                        ps = psB[:, 0:1024] if ping == 0 else psA[:, 0:1024]
                        wait("tensor", sS, l0_exp_last[ping])
                        for j in range(2):
                            kb = 2 * kbp + j
                            emit("tensor", "matmul", ps[:, j * 512:(j + 1) * 512],
                                 ONES_ROW[0:1, 0:128],
                                 nmrow[h % 2][0:1, cq * 512:(cq + 1) * 512],
                                 start=True, stop=False)
                            emit("tensor", "matmul", ps[:, j * 512:(j + 1) * 512],
                                 kh[h % 2][par * 64:(par + 1) * 64,
                                           kb * 128:(kb + 1) * 128],
                                 q0[par * 64:(par + 1) * 64,
                                    hp * T + cq * 512:hp * T + (cq + 1) * 512],
                                 start=False, stop=True)
                        p.tick_last("tensor", sT.tick())
                        wait("scalar", sT, sT.n)
                        emit("scalar", "activation", p0[ping][:, :], ps[:, 0:1024],
                             AF.Exp, inc=sS.tick())
                        l0_exp_last[ping] = sS.n
                        if pend is not None:
                            l0_av(*pend)
                        pend = (ping, kbp, l0_exp_last[ping])
                    l0_av(*pend)
                    p.tick_last("tensor", sT.tick())
                    wait("vector", sT, sT.n)
                    emit("vector", "reciprocal", SMF[0:1, 0:512],
                         psC[64:65, cq * 512:(cq + 1) * 512])
                    emit("vector", "tensor_copy", RBF, SMF[0:1, 0:512],
                         inc=sV.tick())
                    wait("tensor", sV, sV.n)
                    emit("tensor", "matmul", psA[0:64, 1536:2048],
                         ONES_ROW[0:1, 0:64], RBF, start=True, stop=True,
                         inc=sT.tick())
                    wait("vector", sT, sT.n)
                    emit("vector", "tensor_copy", rbuf[:, :], psA[0:64, 1536:2048])
                    emit("vector", "scalar_tensor_tensor",
                         at0[par * 64:(par + 1) * 64,
                             hp * T + cq * 512:hp * T + (cq + 1) * 512],
                         psC[0:64, cq * 512:(cq + 1) * 512], RS512, rbuf[:, :],
                         ALU.mult, ALU.mult, inc=sV.tick())
                    prev_norm_v = sV.n
                done_tok[h] = sT.n
            barrier()
            dump_bf("kh0", kh[0])
            dump_bf("vh0", vh[0])
            dump_bf("nmr0", nmrow[0])
            dump_bf("at0", at0)

        if nphase >= 3:
            gemm_fm(DT["l0_wo_t"], 32, 4, ev_resid,
                    moving=lambda kc, t0, tn: at0[:, kc * T + t0:kc * T + t0 + tn])
            state["xloc"] = "xsp"
            barrier()
            dump_x("x")

        if nphase >= 4:
            ln_phase()
            emit("gpsimd", "dma_start", out=SMF[0:1, 3072:3076],
                 in_=DT["f0_prior"][:, :], inc=sG.tick(16))
            emit("gpsimd", "dma_start", out=SMF[0:1, 3076:3080],
                 in_=DT["f0_rb"][:, :], inc=sG.tick(16))
            wait("scalar", sG, sG.n)
            emit("scalar", "activation", SMF[0:1, 3072:3076], SMF[0:1, 3072:3076],
                 AF.Ln, bias=CST[0:1, 5:6], inc=sS.tick())
            wait("vector", sS, sS.n)
            emit("vector", "tensor_add", SMF[0:1, 3072:3076], SMF[0:1, 3072:3076],
                 SMF[0:1, 3076:3080])
            emit("vector", "tensor_copy", CBR, SMF[0:1, 3072:3076], inc=sV.tick())

            def ev_rt(m, pa):
                wait("vector", sT, sT.n)
                emit("vector", "tensor_copy", lgfm[0:4, :], pa[0:4, 0:T],
                     inc=sV.tick())

            def ex_rt(m, pa):
                wait("tensor", sV, sV.n)
                for t2 in range(2):
                    emit("tensor", "matmul", pa[0:4, t2 * 512:(t2 + 1) * 512],
                         CBR, ONES_ROW[0:1, 0:512],
                         start=False, stop=(t2 == 1))

            gemm_fm(DT["f0_r_t"], 1, 32, ev_rt, bias=False, mw=4, extra=ex_rt)
            for tt in range(8):
                wait("tensor", sV, sV.n)
                emit("tensor", "matmul", psB[0:128, 0:4],
                     lgfm[0:4, tt * 128:(tt + 1) * 128], EYEF[0:4, 0:4],
                     start=True, stop=True, inc=sT.tick())
                wait("vector", sT, sT.n)
                emit("vector", "tensor_copy", lgtm[:, tt * 4:(tt + 1) * 4],
                     psB[0:128, 0:4], inc=sV.tick())
            for tt in range(8):
                sl = lgtm[:, tt * 4:(tt + 1) * 4]
                emit("vector", "reduce_max", SMF[:, 3080 + tt:3081 + tt], sl,
                     mybir.AxisListType.X, negate=True, inc=sV.tick())
                wait("scalar", sV, sV.n)
                emit("scalar", "activation", altm[:, tt * 4:(tt + 1) * 4], sl,
                     AF.Exp, bias=SMF[:, 3080 + tt:3081 + tt], inc=sS.tick())
                wait("vector", sS, sS.n)
                emit("vector", "reduce_sum", SMF[:, 3096:3097],
                     altm[:, tt * 4:(tt + 1) * 4], mybir.AxisListType.X)
                emit("vector", "tensor_copy", SMF[:, 3097:3098], SMF[:, 3096:3097])
                emit("vector", "reciprocal", SMF[:, 3112:3114], SMF[:, 3096:3098],
                     inc=sV.tick())
                wait("scalar", sV, sV.n)
                emit("scalar", "activation", altm[:, tt * 4:(tt + 1) * 4],
                     altm[:, tt * 4:(tt + 1) * 4], AF.Copy,
                     scale=SMF[:, 3112:3113], inc=sS.tick())
                wait("tensor", sS, sS.n)
                emit("tensor", "matmul", psB[0:4, 0:128],
                     altm[:, tt * 4:(tt + 1) * 4], EYEF[:, :],
                     start=True, stop=True, inc=sT.tick())
                wait("vector", sT, sT.n)
                emit("vector", "tensor_copy", alfm[0:4, tt * 128:(tt + 1) * 128],
                     psB[0:4, 0:128], inc=sV.tick())
            wait("tensor", sV, sV.n)
            for t2 in range(2):
                emit("tensor", "matmul", psB[0:32, t2 * 512:(t2 + 1) * 512],
                     SEL2E, alfm[0:4, t2 * 512:(t2 + 1) * 512],
                     start=True, stop=True)
            p.tick_last("tensor", sT.tick())
            wait("vector", sT, sT.n)
            emit("vector", "tensor_copy", absb[:, :], psB[0:32, 0:T], inc=sV.tick())

            def ev_zl(m, pa):
                wait("vector", sT, sT.n)
                emit("vector", "tensor_mul", zl[:, :], pa[0:32, 0:T], absb[:, :],
                     inc=sV.tick())

            gemm_fm(DT["f0_a_t"], 1, 32, ev_zl, bias=False, mw=32)
            emit("gpsimd", "dma_start",
                 out=bass.AP(ballsb, 0, [[4096, 32], [128, 32], [1, 128]]),
                 in_=bass.AP(DT["f0_b_t"], 0, [[128, 32], [32 * 128, 32], [1, 128]]),
                 inc=sG.tick(16))
            wait("tensor", sG, sG.n)
            ball_at = sG.n

            def ex_f0(m, pa):
                for t2 in range(2):
                    emit("tensor", "matmul", pa[:, t2 * 512:(t2 + 1) * 512],
                         ballsb[0:32, m * 128:(m + 1) * 128],
                         zl[0:32, t2 * 512:(t2 + 1) * 512],
                         start=False, stop=(t2 == 1))

            gemm_fm(DT["f0_w_t"], 32, 32, ev_resid, extra=ex_f0)
            barrier()
            if "lgtm" in dbg:
                emit("sync", "dma_start", out=dbg["lgtm"][:, :], in_=lgtm[:, :],
                     inc=sO.tick(16))
            if "altm" in dbg:
                emit("sync", "dma_start", out=dbg["altm"][:, :], in_=altm[:, :],
                     inc=sO.tick(16))
            dump_bf("lgfm", lgfm)
            dump_bf("alfm", alfm)
            dump_x("x")

        if nphase >= 5:
            ln_phase()
            dump_bf("ln", ln)

            def ev_qp(m, pa):
                wait("vector", sT, sT.n)
                emit("vector", "tensor_copy",
                     qpre[:, m * 1028 + 4:m * 1028 + 4 + T], pa[:, 0:T],
                     inc=sV.tick())

            def ev_kp(m, pa):
                wait("vector", sT, sT.n)
                emit("vector", "tensor_copy",
                     kpre[:, m * 1028 + 4:m * 1028 + 4 + T], pa[:, 0:T],
                     inc=sV.tick())

            gemm_fm(DT["l1_wq_t"], 8, 32, ev_qp, bias=False)
            gemm_fm(DT["l1_wk_t"], 2, 32, ev_kp, bias=False)
            vps1 = [psB[:, (kt % 4) * 128 + (kt // 4) * 512:
                        (kt % 4) * 128 + (kt // 4) * 512 + 128] for kt in range(8)]

            def ev_vc(kt, ps):
                emit("vector", "tensor_copy",
                     v1own[:, kt * 256:kt * 256 + 128], ps)

            def ev_vp(kt, ps):
                emit("vector", "tensor_copy",
                     v1own[:, kt * 256 + 128:kt * 256 + 256], ps)

            gemm_tm(DT["l1_wvc_m"], 128, ev_vc, pslots=vps1)
            gemm_tm(DT["l1_wvp_m"], 128, ev_vp, shift=1, pslots=vps1,
                    extra_tok=1023, extra_ps=psC[0:1, 0:128])
            emit("vector", "tensor_copy", SMB[0:1, 2688:2816], psC[0:1, 0:128],
                 inc=sV.tick())
            wait("sync", sV, sV.n)
            emit("sync", "dma_start",
                 out=bass.AP(ccin[2], 0, [[32, 128], [4, 8], [1, 4]]),
                 in_=bass.AP(qpre, 1024, [[8 * 1028, 128], [1028, 8], [1, 4]]),
                 inc=sL.tick(16))
            emit("sync", "dma_start",
                 out=bass.AP(ccin[2], 4096, [[8, 128], [4, 2], [1, 4]]),
                 in_=bass.AP(kpre, 1024, [[2 * 1028, 128], [1028, 2], [1, 4]]),
                 inc=sL.tick(16))
            emit("sync", "dma_start", out=ccin[2][0:1, 5120:5248],
                 in_=SMB[0:1, 2688:2816], inc=sL.tick(16))
            wait("gpsimd", sL, sL.n)
            emit("gpsimd", "collective_compute", "AllGather", ALU.bypass,
                 RG, [ccin[2].ap().opt()], [ccout[2].ap().opt()], inc=sC.tick())
            wait("sync", sC, sC.n)
            emit("sync", "dma_start",
                 out=bass.AP(hstg, 0, [[192, 128], [4, 8], [1, 4]]),
                 in_=bass.AP(ccout[2], 0, [[32, 128], [4, 8], [1, 4]]),
                 inc=sL.tick(16))
            emit("sync", "dma_start",
                 out=bass.AP(hstg, 32, [[192, 128], [4, 2], [1, 4]]),
                 in_=bass.AP(ccout[2], 4096, [[8, 128], [4, 2], [1, 4]]),
                 inc=sL.tick(16))
            emit("sync", "dma_start", out=hstg[0:1, 64:192],
                 in_=ccout[2][0:1, 5120:5248], inc=sL.tick(16))
            wait("vector", sL, sL.n)
            emit("vector", "tensor_scalar_mul",
                 bass.AP(qpre, 0, [[8 * 1028, 128], [1028, 8], [1, 4]]),
                 bass.AP(hstg, 0, [[192, 128], [4, 8], [1, 4]]),
                 CST[:, 3:4])
            emit("vector", "tensor_scalar_mul",
                 bass.AP(kpre, 0, [[2 * 1028, 128], [1028, 2], [1, 4]]),
                 bass.AP(hstg, 32, [[192, 128], [4, 2], [1, 4]]),
                 CST[:, 3:4])
            emit("vector", "tensor_scalar_mul", v1own[0:1, 128:256],
                 hstg[0:1, 64:192], CST[0:1, 3:4], inc=sV.tick())
            barrier()
            dump_bf("qpre", qpre)
            dump_bf("kpre", kpre)
            dump_bf("v1own", v1own)

        if nphase >= 6:
            for c2 in range(2):
                for t2 in range(2):
                    emit("tensor", "matmul",
                         psB[0:64, t2 * 512:(t2 + 1) * 512], SEL4,
                         kpre[:, c2 * 1028 + 4 + t2 * 512:c2 * 1028 + 4 + (t2 + 1) * 512],
                         start=(c2 == 0), stop=(c2 == 1))
            for c8 in range(8):
                for t2 in range(2):
                    emit("tensor", "matmul",
                         psC[0:64, t2 * 512:(t2 + 1) * 512], SEL16,
                         qpre[:, c8 * 1028 + 4 + t2 * 512:c8 * 1028 + 4 + (t2 + 1) * 512],
                         start=(c8 == 0), stop=(c8 == 7))
            p.tick_last("tensor", sT.tick())
            wait("vector", sT, sT.n)
            emit("vector", "tensor_copy", kmb[:, :], psB[0:64, 0:T])
            emit("vector", "tensor_copy", qmb[:, :], psC[0:64, 0:T], inc=sV.tick())

            def conv1(src, dst, nch_, slab_dram):
                sg_prev = [sT.n, sT.n]
                for mc in range(nch_):
                    bi = mc % 2
                    wait("gpsimd", sT, sg_prev[bi])
                    emit("gpsimd", "dma_start",
                         out=bass.AP(wslc[bi], 0, [[25 * 128, 128], [128, 3], [1, 128]]),
                         in_=bass.AP(slab_dram, mc * 385 * 128,
                                     [[128, 128], [16384, 3], [1, 128]]),
                         inc=sG.tick(16))
                    emit("gpsimd", "dma_start",
                         out=wslc[bi][0:1, 384:512],
                         in_=slab_dram[mc, 384:385, :], inc=sG.tick(16))
                    wait("tensor", sG, sG.n)
                    for (r0, rn) in ((0, 512), (512, 512), (1024, 2)):
                        slot = psA[:, 0:512] if (r0 // 512) % 2 == 0 else psA[:, 512:1024]
                        if r0 == 1024:
                            slot = psA[:, 1024:1026]
                        for j in range(3):
                            emit("tensor", "matmul", slot[:, 0:rn],
                                 wslc[bi][:, j * 128:(j + 1) * 128],
                                 src[:, mc * 1028 + r0 + j:mc * 1028 + r0 + j + rn],
                                 start=(j == 0), stop=False)
                        emit("tensor", "matmul", slot[:, 0:rn],
                             wslc[bi][0:1, 384:512], ONES_ROW[0:1, 0:rn],
                             start=False, stop=True)
                        p.tick_last("tensor", sT.tick())
                        wait("vector", sT, sT.n)
                        emit("vector", "tensor_copy",
                             dst[:, mc * 1026 + r0:mc * 1026 + r0 + rn],
                             slot[:, 0:rn], inc=sV.tick())
                        wait("tensor", sV, sV.n)
                    sg_prev[bi] = sT.n
                emit("vector", "tensor_scalar_mul",
                     bass.AP(dst, 0, [[nch_ * 1026, 128], [1026, nch_], [1, 2]]),
                     bass.AP(dst, 0, [[nch_ * 1026, 128], [1026, nch_], [1, 2]]),
                     CST[:, 3:4], inc=sV.tick())

            conv1(qpre, qc1o, 8, DT["qc1_s"])
            conv1(kpre, kc1o, 2, DT["kc1_s"])
            dump_bf("qc1o", qc1o)

            def conv2(src, dst, nch_, slab_dram, meanb, is_k):
                nkc = nch_
                sg_prev = [sT.n, sT.n]
                c2_ent_v, c2_ent_s = sV.n, sS.n
                for mc in range(nch_):
                    bi = mc % 2
                    wait("gpsimd", sT, sg_prev[bi])
                    nrows = 3 * nkc * 128
                    R_ = slab_dram.shape[1]
                    emit("gpsimd", "dma_start",
                         out=bass.AP(wslc[bi], 0, [[25 * 128, 128], [128, 3 * nkc], [1, 128]]),
                         in_=bass.AP(slab_dram, mc * R_ * 128,
                                     [[128, 128], [16384, 3 * nkc], [1, 128]]),
                         inc=sG.tick(16))
                    emit("gpsimd", "dma_start",
                         out=wslc[bi][0:1, 3 * nkc * 128:3 * nkc * 128 + 128],
                         in_=slab_dram[mc, nrows:nrows + 1, :], inc=sG.tick(16))
                    wait("tensor", sG, sG.n)
                    slot = psA[:, (mc % 2) * 1024:(mc % 2) * 1024 + 1024]
                    if mc >= 2:
                        wait("tensor", sV, vfree[mc - 2])
                        wait("tensor", sS, sfree[mc - 2])
                    else:
                        wait("tensor", sV, c2_ent_v)
                        wait("tensor", sS, c2_ent_s)
                    for j in range(3):
                        for kc in range(nkc):
                            for t2 in range(2):
                                emit("tensor", "matmul",
                                     slot[:, t2 * 512:(t2 + 1) * 512],
                                     wslc[bi][:, (j * nkc + kc) * 128:(j * nkc + kc + 1) * 128],
                                     src[:, kc * 1026 + t2 * 512 + j:
                                         kc * 1026 + t2 * 512 + j + 512],
                                     start=(j == 0 and kc == 0), stop=False)
                    for t2 in range(2):
                        emit("tensor", "matmul", slot[:, t2 * 512:(t2 + 1) * 512],
                             wslc[bi][0:1, 3 * nkc * 128:3 * nkc * 128 + 128],
                             ONES_ROW[0:1, 0:512], start=False, stop=False)
                        emit("tensor", "matmul", slot[:, t2 * 512:(t2 + 1) * 512],
                             DUP64, meanb[0:64, t2 * 512:(t2 + 1) * 512],
                             start=False, stop=(t2 == 1))
                    p.tick_last("tensor", sT.tick())
                    sg_prev[bi] = sT.n
                    wait("scalar", sT, sT.n)
                    emit("scalar", "activation", q2r[:, :], slot[:, 0:T],
                         AF.Square, inc=sS.tick())
                    wait("tensor", sS, sS.n)
                    for t2 in range(2):
                        emit("tensor", "matmul", psC[0:2, t2 * 512:(t2 + 1) * 512],
                             SEL2H, q2r[:, t2 * 512:(t2 + 1) * 512],
                             start=True, stop=True)
                    p.tick_last("tensor", sT.tick())
                    wait("vector", sT, sT.n)
                    emit("vector", "tensor_scalar_max", SMF[0:2, 0:1024],
                         psC[0:2, 0:T], 1e-24, inc=sV.tick())
                    wait("scalar", sV, sV.n)
                    emit("scalar", "activation", SMF[0:2, 0:1024], SMF[0:2, 0:1024],
                         AF.Ln)
                    emit("scalar", "activation", SMF[0:2, 0:1024], SMF[0:2, 0:1024],
                         AF.Exp, scale=CST[0:2, 1:2], inc=sS.tick())
                    wait("vector", sS, sS.n)
                    if is_k:
                        emit("vector", "tensor_scalar_mul", SMF[0:2, 0:1024],
                             SMF[0:2, 0:1024], CST[0:2, 8:9])
                    emit("vector", "tensor_copy", RSB2, SMF[0:2, 0:1024],
                         inc=sV.tick())
                    wait("tensor", sV, sV.n)
                    for t2 in range(2):
                        emit("tensor", "matmul", psC[:, t2 * 512:(t2 + 1) * 512],
                             BAND2, RSB2[0:2, t2 * 512:(t2 + 1) * 512],
                             start=True, stop=True)
                    p.tick_last("tensor", sT.tick())
                    wait("vector", sT, sT.n)
                    emit("vector", "tensor_copy", rstb[:, :], psC[:, 0:T])
                    emit("vector", "tensor_mul", qhat[:, :], slot[:, 0:T],
                         rstb[:, :], inc=sV.tick())
                    vfree[mc] = sV.n
                    sfree[mc] = sS.n
                    mask = []
                    for i2 in range(0, 32, 2):
                        mask += [i2 + 1, i2]
                    emit("vector", "stream_shuffle", qswp[:, :], qhat[:, :], mask)
                    emit("vector", "tensor_mul", qt1[:, :], qhat[:, :], RPC[:, :])
                    emit("vector", "tensor_mul", qswp[:, :], qswp[:, :], RPS[:, :])
                    emit("vector", "tensor_add", dst[:, mc * T:(mc + 1) * T],
                         qt1[:, :], qswp[:, :], inc=sV.tick())

            vfree = {}
            sfree = {}
            conv2(qc1o, qfin, 8, DT["qc2_s"], kmb, False)
            vfree = {}
            sfree = {}
            conv2(kc1o, kfin, 2, DT["kc2_s"], qmb, True)
            wait("sync", sV, sV.n)
            emit("sync", "dma_start",
                 out=bass.AP(ccin[3], 0, [[2048, 128], [1, 2048]]),
                 in_=kfin[:, :], inc=sL.tick(16))
            emit("sync", "dma_start",
                 out=bass.AP(ccin[3], 256 * T, [[2048, 128], [1, 2048]]),
                 in_=v1own[:, :], inc=sL.tick(16))
            wait("gpsimd", sL, sL.n)
            emit("gpsimd", "collective_compute", "AllGather", ALU.bypass,
                 RG, [ccin[3].ap().opt()], [ccout[3].ap().opt()], inc=sC.tick())
            barrier()
            dump_bf("qfin", qfin)
            dump_bf("kfin", kfin)

        if nphase >= 7:
            wait("sync", sC, sC.n)
            for vv in (vg[0], vg[1]):
                emit("vector", "memset",
                     bass.AP(vv, 64, [[16 * 65, P], [65, 16], [1, 1]]), 1.0)
            emit("vector", "tensor_copy", SMF[0:1, 3132:3133],
                 SMF[0:1, 3130:3131], inc=sV.tick())
            g_at = {}
            g_done = {}

            def l1_load(g):
                if g >= 2:
                    wait("sync", sT, g_done[g - 2])
                pg, cg = g % 2, g // 2
                for s in range(2):
                    for par2 in range(2):
                        emit("sync", "dma_start",
                             out=kg[g % 2][par2 * 64:(par2 + 1) * 64,
                                           s * 1024:(s + 1) * 1024],
                             in_=bass.AP(ccout[3], s * X3 + (pg * 64) * 2048 + cg * 1024,
                                         [[2048, 64], [1, 1024]]),
                             inc=sL.tick(16))
                    emit("sync", "dma_start",
                         out=bass.AP(vg[g % 2], s * 8 * 65,
                                     [[16 * 65, 128], [65, 8], [1, 64]]),
                         in_=bass.AP(ccout[3], s * X3 + 256 * T + g * 64,
                                     [[2048, 128], [256, 8], [1, 64]]),
                         inc=sL.tick(16))
                g_at[g] = sL.n

            MASKED = {0: list(range(0, 12)), 1: [4, 5, 6, 7, 12, 13, 14, 15]}
            l1_load(0)
            prev_nv = 0
            l1_exp_last = {0: 0, 1: 0}
            l1_msk_last = {0: 0, 1: 0}
            for g in range(4):
                if g + 1 < 4:
                    l1_load(g + 1)
                wait("tensor", sL, g_at[g])
                for h in range(g * 4, g * 4 + 4):
                    par, hp = h % 2, h // 2
                    for cq in range(2):
                        nkb = 4 * cq + 12
                        npair = nkb // 2

                        def l1_av(ping_, kbp_, exp_tick, mv_, srcs_):
                            wait("tensor", sS, exp_tick)
                            if mv_:
                                wait("tensor", sV, mv_)
                            if kbp_ == 0:
                                wait("tensor", sV, prev_nv)
                            for j in range(2):
                                emit("tensor", "matmul",
                                     psC[0:65, cq * 512:(cq + 1) * 512],
                                     vg[g % 2][:, (2 * kbp_ + j) * 65:
                                               (2 * kbp_ + j + 1) * 65],
                                     srcs_[j][:, j * 512:(j + 1) * 512],
                                     start=(kbp_ == 0 and j == 0),
                                     stop=(kbp_ == npair - 1 and j == 1))

                        pend = None
                        for kbp in range(npair):
                            ping = kbp % 2
                            ps = psB[:, 0:1024] if ping == 0 else psA[:, 0:1024]
                            wait("tensor", sS, l1_exp_last[ping])
                            wait("tensor", sV, l1_msk_last[ping])
                            for j in range(2):
                                kb = 2 * kbp + j
                                first = True
                                if kb >= 8:
                                    emit("tensor", "matmul",
                                         ps[:, j * 512:(j + 1) * 512],
                                         KB_ROW[0:1, kb * 128:(kb + 1) * 128],
                                         ONES_ROW[0:1, 0:512],
                                         start=True, stop=False)
                                    first = False
                                emit("tensor", "matmul", ps[:, j * 512:(j + 1) * 512],
                                     kg[g % 2][par * 64:(par + 1) * 64,
                                               kb * 128:(kb + 1) * 128],
                                     qfin[par * 64:(par + 1) * 64,
                                          hp * T + cq * 512:hp * T + (cq + 1) * 512],
                                     start=first, stop=True)
                            p.tick_last("tensor", sT.tick())
                            wait("scalar", sT, sT.n)
                            wait("scalar", sV, l1_msk_last[ping])
                            emit("scalar", "activation", p1b[ping][:, :],
                                 ps[:, 0:1024], AF.Exp, scale=CST[:, 2:3],
                                 inc=sS.tick())
                            l1_exp_last[ping] = sS.n
                            mv = 0
                            srcs = [None, None]
                            for j in range(2):
                                kb = 2 * kbp + j
                                if kb in MASKED[cq]:
                                    idx = cq * 512 - 128 * kb + 1408
                                    wait("vector", sS, sS.n)
                                    emit("vector", "tensor_mul",
                                         pm1[ping][:, j * 512:(j + 1) * 512],
                                         p1b[ping][:, j * 512:(j + 1) * 512],
                                         STRP[:, idx:idx + 512], inc=sV.tick())
                                    srcs[j] = pm1[ping]
                                    mv = sV.n
                                    l1_msk_last[ping] = sV.n
                                else:
                                    srcs[j] = p1b[ping]
                            if pend is not None:
                                l1_av(*pend)
                            pend = (ping, kbp, l1_exp_last[ping], mv, srcs)
                        l1_av(*pend)
                        p.tick_last("tensor", sT.tick())
                        wait("vector", sT, sT.n)
                        emit("vector", "reciprocal", SMF[0:1, 0:512],
                             psC[64:65, cq * 512:(cq + 1) * 512])
                        emit("vector", "tensor_copy", RBF, SMF[0:1, 0:512],
                             inc=sV.tick())
                        wait("tensor", sV, sV.n)
                        emit("tensor", "matmul", psA[0:64, 1536:2048],
                             ONES_ROW[0:1, 0:64], RBF, start=True, stop=True,
                             inc=sT.tick())
                        wait("vector", sT, sT.n)
                        emit("vector", "tensor_copy", rb1[:, :],
                             psA[0:64, 1536:2048])
                        emit("vector", "scalar_tensor_tensor",
                             at1[par * 64:(par + 1) * 64,
                                 hp * T + cq * 512:hp * T + (cq + 1) * 512],
                             psC[0:64, cq * 512:(cq + 1) * 512], 1.0, rb1[:, :],
                             ALU.mult, ALU.mult, inc=sV.tick())
                        prev_nv = sV.n
                g_done[g] = sT.n
            barrier()
            dump_bf("kg0", kg[0])
            dump_bf("vg0", vg[0])
            dump_bf("at1", at1)

        if nphase >= 8:
            gemm_fm(DT["l1_wo_t"], 32, 8, ev_resid, bias=False,
                    moving=lambda kc, t0, tn: at1[:, kc * T + t0:kc * T + t0 + tn])
            barrier()
            dump_x("x")

        if nphase >= 9:
            ln_phase()

            def ev_u1(m, pa):
                wait("vector", sT, sT.n)
                emit("vector", "tensor_scalar", u1[:, :], pa[:, 0:T],
                     PRP[:, 0:1], None, ALU.mult, inc=sV.tick())

            gemm_fm(DT["f1_a_t"], 1, 32, ev_u1, bias=False)
            dump_bf("u1", u1)

            def ev_z(m, pa):
                wait("scalar", sT, sT.n)
                emit("scalar", "activation", rot[m % 2][:, :], pa[:, 0:T],
                     AF.Square, inc=sS.tick())
                wait("tensor", sS, sS.n)
                for t2 in range(2):
                    emit("tensor", "matmul", psC[0:1, t2 * 512:(t2 + 1) * 512],
                         ONES_COL, rot[m % 2][:, t2 * 512:(t2 + 1) * 512],
                         start=(m == 0), stop=(m == 31))
                p.tick_last("tensor", sT.tick())
                ev_resid(m, pa)

            gemm_fm(DT["f1_w_t"], 32, 32, ev_z)

            def ev_m1(m, pa):
                wait("scalar", sT, sT.n)
                emit("scalar", "activation", rot[m % 2][:, :], pa[:, 0:T],
                     AF.Square, inc=sS.tick())
                wait("tensor", sS, sS.n)
                for t2 in range(2):
                    emit("tensor", "matmul", psC[32:33, t2 * 512:(t2 + 1) * 512],
                         ONES_COL, rot[m % 2][:, t2 * 512:(t2 + 1) * 512],
                         start=(m == 0), stop=(m == 31))
                p.tick_last("tensor", sT.tick())

            gemm_fm(DT["f1_b_t"], 32, 1, ev_m1, bias=False,
                    moving=lambda kc, t0, tn: u1[:, t0:t0 + tn])
            wait("scalar", sT, sT.n)
            NZ = SMF[0:1, 0:1024]
            NM = SMF[0:1, 1024:2048]
            RC = SMF[0:1, 2048:3072]
            emit("scalar", "activation", NZ, psC[0:1, 0:T], AF.Ln)
            emit("scalar", "activation", NZ, NZ, AF.Exp, scale=CST[0:1, 7:8])
            emit("scalar", "activation", NM, psC[32:33, 0:T], AF.Ln)
            emit("scalar", "activation", NM, NM, AF.Exp,
                 scale=CST[0:1, 7:8], inc=sS.tick())
            wait("vector", sS, sS.n)
            emit("vector", "tensor_scalar_add", NM, NM, 1e-6)
            emit("vector", "reciprocal", RC, NM)
            emit("vector", "tensor_mul", NM, NZ, RC)
            emit("vector", "tensor_scalar", NM, NM, 0.5, 1.0,
                 ALU.mult, ALU.min)
            emit("vector", "tensor_copy", GBF, NM, inc=sV.tick())
            wait("tensor", sV, sV.n)
            for t2 in range(2):
                emit("tensor", "matmul", psB[:, t2 * 512:(t2 + 1) * 512],
                     ONES_ROW[0:1, 0:128], GBF[0:1, t2 * 512:(t2 + 1) * 512],
                     start=True, stop=True)
            p.tick_last("tensor", sT.tick())
            wait("vector", sT, sT.n)
            emit("vector", "tensor_copy", gm_bc[:, :], psB[:, 0:T], inc=sV.tick())

            def ev_m2(m, pa):
                wait("vector", sT, sT.n)
                emit("vector", "tensor_mul", tfr[:, :], pa[:, 0:T], gm_bc[:, :],
                     inc=sV.tick())
                ev_resid_add_tfr(m)

            def ev_resid_add_tfr(m):
                if m < NRES:
                    emit("vector", "tensor_add", xres[:, m * T:(m + 1) * T],
                         xres[:, m * T:(m + 1) * T], tfr[:, :], inc=sV.tick())
                    wait("sync", sV, sV.n)
                    emit("sync", "dma_start", out=yout[m * 128:(m + 1) * 128, :],
                         in_=xres[:, m * T:(m + 1) * T], inc=sO.tick(16))
                else:
                    wait("sync", sV, sV.n)
                    emit("sync", "dma_start", out=xstg[:, :], in_=xsp_read_ap(m),
                         inc=sL.tick(16))
                    wait("vector", sL, sL.n)
                    emit("vector", "tensor_add", xstg[:, :], xstg[:, :], tfr[:, :],
                         inc=sV.tick())
                    wait("sync", sV, sV.n)
                    emit("sync", "dma_start", out=yout[m * 128:(m + 1) * 128, :],
                         in_=xstg[:, :], inc=sO.tick(16))

            gemm_fm(DT["f1_b_t"], 32, 1, ev_m2, bias=False,
                    moving=lambda kc, t0, tn: u1[:, t0:t0 + tn])

        if nphase < 9:
            barrier()
            for m in range(NCH):
                if m < NRES:
                    emit("sync", "dma_start", out=yout[m * 128:(m + 1) * 128, :],
                         in_=xres[:, m * T:(m + 1) * T], inc=sO.tick(16))
                else:
                    wait("sync", sO, sO.n)
                    emit("sync", "dma_start", out=xstg[:, :], in_=xsp_read_ap(m),
                         inc=sL.tick(16))
                    wait("sync", sL, sL.n)
                    emit("sync", "dma_start", out=yout[m * 128:(m + 1) * 128, :],
                         in_=xstg[:, :], inc=sO.tick(16))

        wait("sync", sO, sO.n)

        with nc.Block() as block:
            @block.tensor
            def _(h):
                for f in p.q["tensor"]:
                    f.run(h)

            @block.vector
            def _(h):
                for f in p.q["vector"]:
                    f.run(h)

            @block.scalar
            def _(h):
                for f in p.q["scalar"]:
                    f.run(h)

            @block.gpsimd
            def _(h):
                for f in p.q["gpsimd"]:
                    f.run(h)

            @block.sync
            def _(h):
                for f in p.q["sync"]:
                    f.run(h)

    return nc


_PROG_CACHE = {}


def kernel(**inputs):
    in_maps = host_prepare(inputs)
    key = ("full", 9)
    if key not in _PROG_CACHE:
        _PROG_CACHE[key] = build_program(dumps=(), nphase=9)
    nc = _PROG_CACHE[key]
    res = run_bass_kernel_spmd(nc, in_maps, core_ids=list(range(8)))
    out = np.empty((4, 2048, 4096), np.float32)
    for c in range(8):
        b, hf = c // 2, c % 2
        out[b, hf * T:(hf + 1) * T, :] = res.results[c]["yout"].T
    return out
